# version 03
# baseline (speedup 1.0000x reference)
import contextlib

import numpy as np
import ml_dtypes

import concourse.bass as bass
import concourse.mybir as mybir
from concourse.bass_utils import run_bass_kernel_spmd

F32 = mybir.dt.float32
BF16 = mybir.dt.bfloat16
F8E5 = mybir.dt.float8e5
AF = mybir.ActivationFunctionType
ALU = mybir.AluOpType

P = 128
D = 1024
T = 2048
NB = 2
NST = 16
HQ = 1024
TH = 512
S2 = 256
DF = 4096
NCORE = 8
EPS = 1e-5


class Ctr:
    def __init__(self, sem, step):
        self.sem = sem
        self.n = 0
        self.step = step

    def inc(self, instr):
        instr.then_inc(self.sem, self.step)
        self.n += self.step
        return self.n


def build_program():
    nc = bass.Bass("TRN2", target_bir_lowering=False, debug=False)
    dt = nc.dram_tensor

    x_fm = dt("x_fm", [NB, D, T], F32, kind="ExternalInput").ap()
    w_in = dt("w_in", [D, 2 * D], F32, kind="ExternalInput").ap()
    w_out = dt("w_out", [D, D], F32, kind="ExternalInput").ap()
    wxp = dt("wxp", [D, 65], F32, kind="ExternalInput").ap()
    w1t = dt("w1t", [D, DF], F32, kind="ExternalInput").ap()
    w2t = dt("w2t", [DF, D], F32, kind="ExternalInput").ap()
    loga_r = dt("loga_r", [P, 128], F32, kind="ExternalInput").ap()
    dgconv = dt("dgconv", [8, 4, P, P], F32, kind="ExternalInput").ap()
    dgd = dt("dgd", [8, P, P], F32, kind="ExternalInput").ap()
    cbcol = dt("cbcol", [P, 8], F32, kind="ExternalInput").ap()
    b1r = dt("b1r", [P, 32], F32, kind="ExternalInput").ap()
    b2r = dt("b2r", [P, 8], F32, kind="ExternalInput").ap()
    mask_fm = dt("mask_fm", [NB, D, T], BF16, kind="ExternalInput").ap()
    i128 = dt("i128", [P, P], BF16, kind="ExternalInput").ap()
    onesln = dt("onesln", [P, 1], BF16, kind="ExternalInput").ap()

    out_fm = dt("out_fm", [NB, D, T], F32, kind="ExternalOutput").ap()
    xhat_scr = dt("xhat_scr", [NB, D, T], F32, kind="Internal").ap()
    scr_b = dt("scr_b", [NB, 2, NST, HQ], BF16, kind="Internal").ap()
    scr_c = dt("scr_c", [NB, 2, NST, HQ], BF16, kind="Internal").ap()
    scr_d = dt("scr_d", [NB, 2, 1, HQ], F32, kind="Internal").ap()
    row_scr = dt("row_scr", [1, 2 * TH], F32, kind="Internal").ap()
    row_scr2 = dt("row_scr2", [1, 2 * S2], F32, kind="Internal").ap()

    st = contextlib.ExitStack()
    with st:
        e = st.enter_context
        dS = Ctr(e(nc.semaphore("dS")), 16)
        dG = Ctr(e(nc.semaphore("dG")), 16)
        sE = Ctr(e(nc.semaphore("sE")), 1)
        vE = Ctr(e(nc.semaphore("vE")), 1)
        tE = Ctr(e(nc.semaphore("tE")), 1)
        block = e(nc.Block())

        def barrier():
            waits = [(dS.sem, dS.n), (dG.sem, dG.n), (sE.sem, sE.n),
                     (vE.sem, vE.n), (tE.sem, tE.n)]

            def emit(eng):
                for sem, n in waits:
                    eng.wait_ge(sem, n)
            block.sync(emit)
            block.gpsimd(emit)
            block.scalar(emit)
            block.vector(emit)
            block.tensor(emit)

        a_sb = e(nc.sbuf_tensor([P, 128], F32))
        b1q_sb = e(nc.sbuf_tensor([P, 32], F32))
        b2q_sb = e(nc.sbuf_tensor([P, 8], F32))
        b1f8 = e(nc.sbuf_tensor([P, 32], F8E5))
        b2f8 = e(nc.sbuf_tensor([P, 8], F8E5))
        i128_sb = e(nc.sbuf_tensor([P, P], BF16))
        ones_sb = e(nc.sbuf_tensor([P, 1], BF16))
        eps_sb = e(nc.sbuf_tensor([P, 1], F32))
        pp = [e(nc.psum_tensor(f"pp{i}", [P, 512], F32)) for i in range(8)]

        with contextlib.ExitStack() as s1:
            e1 = s1.enter_context
            dg_sb = e1(nc.sbuf_tensor([P, 8, 4, P], BF16))
            dd_sb = e1(nc.sbuf_tensor([P, 8, P], BF16))
            wxp_sb = e1(nc.sbuf_tensor([P, 8, 65], BF16))
            cb_sb = e1(nc.sbuf_tensor([P, 8], F32))
            loga_sb = e1(nc.sbuf_tensor([P, 128], F32))

            x_bf = e1(nc.sbuf_tensor([P, 8, HQ], BF16))
            ubuf = e1(nc.sbuf_tensor([P, 8 * (HQ + 3)], BF16))
            halo = e1(nc.sbuf_tensor([P, 8, 3], BF16))
            x_conv = e1(nc.sbuf_tensor([P, 8, HQ], BF16))
            silu_z = e1(nc.sbuf_tensor([P, 8, HQ], BF16))
            dlb = e1(nc.sbuf_tensor([P, HQ], F32))
            b_bf = e1(nc.sbuf_tensor([NST, HQ], BF16))
            bd16 = e1(nc.sbuf_tensor([NST, HQ], F32))
            bdel_bf = e1(nc.sbuf_tensor([NST, HQ], BF16))
            c_bf = e1(nc.sbuf_tensor([NST, HQ], BF16))
            erow = e1(nc.sbuf_tensor([1, HQ], F32))
            drow = e1(nc.sbuf_tensor([1, HQ], F32))
            uf32 = e1(nc.sbuf_tensor([P, 8 * TH], F32))
            db_b = [e1(nc.sbuf_tensor(f"db{i}", [P, HQ], BF16)) for i in range(2)]
            h_b = [e1(nc.sbuf_tensor(f"hb{i}", [P, HQ], BF16)) for i in range(2)]
            z_b = [e1(nc.sbuf_tensor(f"zb{i}", [P, HQ], BF16)) for i in range(2)]
            hcar = e1(nc.sbuf_tensor([P, 128], F32))
            y_sb = e1(nc.sbuf_tensor([P, 8, HQ], BF16))
            t3 = e1(nc.sbuf_tensor([P, 8, HQ], BF16))
            win_b = [e1(nc.sbuf_tensor(f"win{i}", [P, 8, 128], BF16)) for i in range(2)]
            wo_b = [e1(nc.sbuf_tensor(f"wo{i}", [P, 8, 128], BF16)) for i in range(2)]
            xres = e1(nc.sbuf_tensor([P, 8, TH], F32))
            d1t = e1(nc.sbuf_tensor([P, TH], F32))
            rbf = e1(nc.sbuf_tensor([P, 2, TH], BF16))
            sbf = e1(nc.sbuf_tensor([P, 2, TH], BF16))
            mrow = e1(nc.sbuf_tensor([1, 2 * TH], F32))
            vrow = e1(nc.sbuf_tensor([1, TH], F32))
            srow = e1(nc.sbuf_tensor([1, TH], F32))
            mrb = e1(nc.sbuf_tensor([P, TH], F32))
            rsb = e1(nc.sbuf_tensor([P, TH], F32))

            xm = ubuf[:].rearrange("p (g t) -> p g t", g=8)
            bb_blk = ubuf[:, 0:4 * HQ].rearrange("p (n t) -> p n t", n=4)
            cb_blk = ubuf[:, 4 * HQ:8 * HQ].rearrange("p (n t) -> p n t", n=4)
            res = uf32[:].rearrange("p (g t) -> p g t", g=8)
            da_v = [uf32[:, 0:HQ], uf32[:, HQ:2 * HQ]]

            def g_(g):
                dG.inc(g.dma_start(out=dg_sb[:],
                                   in_=dgconv.rearrange("g j k p -> k g j p")))
                dG.inc(g.dma_start(out=dd_sb[:],
                                   in_=dgd.rearrange("g k p -> k g p")))
                dG.inc(g.dma_start(out=wxp_sb[:],
                                   in_=wxp.rearrange("(g p) o -> p g o", p=P)))
                dG.inc(g.dma_start(out=b1f8[:], in_=b1r))
                dG.inc(g.dma_start(out=b2f8[:], in_=b2r))
            block.gpsimd(g_)

            def y0_(y):
                dS.inc(y.dma_start(out=loga_sb[:], in_=loga_r))
                dS.inc(y.dma_start(out=cb_sb[:], in_=cbcol))
                dS.inc(y.dma_start(out=i128_sb[:], in_=i128))
                dS.inc(y.dma_start(out=ones_sb[:], in_=onesln))
            block.sync(y0_)

            def s0_(s):
                s.wait_ge(dS.sem, dS.n)
                s.wait_ge(dG.sem, dG.n)
                s.activation(out=a_sb[:], in_=loga_sb[:], func=AF.Exp)
                s.copy(out=b1q_sb[:], in_=b1f8[:])
                ii = s.copy(out=b2q_sb[:], in_=b2f8[:])
                sE.inc(ii)
            block.scalar(s0_)

            def v0_(v):
                v.wait_ge(sE.sem, sE.n)
                v.memset(eps_sb[:], EPS)
                ii = v.tensor_scalar_mul(out=a_sb[:], in0=a_sb[:], scalar1=-1.0)
                vE.inc(ii)
            block.vector(v0_)

            def front(b, q):
                tok0 = q * HQ
                tE_base = tE.n
                sE_base = sE.n

                def g_(g):
                    ii = g.dma_start(
                        out=x_bf[:],
                        in_=x_fm[b].rearrange("(g p) t -> p g t", p=P)[:, :, tok0:tok0 + HQ])
                    dG.inc(ii)
                    for ob in range(16):
                        if ob >= 2:
                            g.wait_ge(tE.sem, tE_base + 2 * (ob - 2) + 2)
                        ii = g.dma_start(
                            out=win_b[ob % 2][:],
                            in_=w_in.rearrange("(g p) o -> p g o", p=P)[:, :, ob * 128:(ob + 1) * 128])
                        dG.inc(ii)
                block.gpsimd(g_)
                x_d = dG.n - 16 * 16
                win_d = [x_d + 16 * (ob + 1) for ob in range(16)]

                def v_(v):
                    if q == 0:
                        ii = v.memset(xm[:, :, 0:3], 0.0)
                    else:
                        ii = v.tensor_copy(out=xm[:, :, 0:3], in_=halo[:])
                    vE.inc(ii)
                block.vector(v_)
                halo_v = vE.n

                def t_(t):
                    t.wait_ge(dG.sem, x_d)
                    t.wait_ge(vE.sem, halo_v)
                    for ob in range(16):
                        t.wait_ge(dG.sem, win_d[ob])
                        for th in range(2):
                            k = 2 * ob + th
                            if k >= 2:
                                t.wait_ge(sE.sem, sE_base + k - 1)
                            for g in range(8):
                                ii = t.matmul(pp[k % 2][:, :], win_b[ob % 2][:, g, :],
                                              x_bf[:, g, th * 512:(th + 1) * 512],
                                              start=(g == 0), stop=(g == 7))
                            tE.inc(ii)
                block.tensor(t_)

                def s_(s):
                    s.wait_ge(vE.sem, halo_v)
                    for ob in range(16):
                        for th in range(2):
                            k = 2 * ob + th
                            s.wait_ge(tE.sem, tE_base + k + 1)
                            if ob < 8:
                                ii = s.copy(out=xm[:, ob, 3 + th * 512: 3 + (th + 1) * 512],
                                            in_=pp[k % 2][:, :])
                            else:
                                ii = s.activation(out=silu_z[:, ob - 8, th * 512:(th + 1) * 512],
                                                  in_=pp[k % 2][:, :], func=AF.Silu)
                            sE.inc(ii)
                block.scalar(s_)
                xz_evac = sE.n

                def v1_(v):
                    v.wait_ge(sE.sem, xz_evac)
                    ii = v.tensor_copy(out=halo[:], in_=xm[:, :, HQ:HQ + 3])
                    vE.inc(ii)
                block.vector(v1_)
                halo_saved = vE.n

                tE_c = tE.n
                sE_c = sE.n

                def t2_(t):
                    t.wait_ge(sE.sem, xz_evac)
                    for g in range(8):
                        for th in range(2):
                            k = 2 * g + th
                            if k >= 2:
                                t.wait_ge(sE.sem, sE_c + k - 1)
                            for j in range(4):
                                ii = t.matmul(pp[k % 2][:, :], dg_sb[:, g, j, :],
                                              xm[:, g, th * 512 + j: th * 512 + j + 512],
                                              start=(j == 0), stop=(j == 3))
                            tE.inc(ii)
                block.tensor(t2_)

                def s2_(s):
                    for g in range(8):
                        for th in range(2):
                            k = 2 * g + th
                            s.wait_ge(tE.sem, tE_c + k + 1)
                            ii = s.activation(out=x_conv[:, g, th * 512:(th + 1) * 512],
                                              in_=pp[k % 2][:, :], func=AF.Silu,
                                              bias=cb_sb[:, g:g + 1], scale=1.0)
                            sE.inc(ii)
                block.scalar(s2_)
                conv_evac = sE.n

                proj_t = {}
                proj_s = {}
                for th in range(2):
                    def tp_(t, th=th):
                        if th == 0:
                            t.wait_ge(sE.sem, conv_evac)
                        else:
                            t.wait_ge(sE.sem, proj_s[0])
                        for g in range(8):
                            ii = t.matmul(pp[2][0:65, :], wxp_sb[:, g, :],
                                          x_conv[:, g, th * 512:(th + 1) * 512],
                                          start=(g == 0), stop=(g == 7))
                        tE.inc(ii)
                    block.tensor(tp_)
                    proj_t[th] = tE.n

                    def sp_(s, th=th):
                        s.wait_ge(tE.sem, proj_t[th])
                        sl = slice(th * 512, (th + 1) * 512)
                        s.copy(out=b_bf[:, sl], in_=pp[2][0:16, :])
                        s.copy(out=c_bf[:, sl], in_=pp[2][32:48, :])
                        ii = s.activation(out=erow[:, sl], in_=pp[2][64:65, :], func=AF.Exp)
                        sE.inc(ii)
                    block.scalar(sp_)
                    proj_s[th] = sE.n

                def s3_(s):
                    ii = s.activation(out=drow[:], in_=erow[:], func=AF.Ln, bias=1.0)
                    sE.inc(ii)
                block.scalar(s3_)
                drow_s = sE.n

                def y_(y):
                    y.wait_ge(sE.sem, drow_s)
                    ii = y.dma_start(out=scr_d[b, q], in_=drow[:])
                    dS.inc(ii)
                    y.wait_ge(dS.sem, dS.n)
                    ii = y.dma_start(
                        out=dlb[:],
                        in_=scr_d[b, q].rearrange("a t -> (a t)").partition_broadcast(P))
                    dS.inc(ii)
                    ii = y.dma_start(
                        out=bd16[:],
                        in_=scr_d[b, q].rearrange("a t -> (a t)").partition_broadcast(NST))
                    dS.inc(ii)
                block.sync(y_)
                dlb_d = dS.n - 16
                bd16_d = dS.n

                def v2_(v):
                    v.wait_ge(dS.sem, bd16_d)
                    ii = v.tensor_tensor(out=bdel_bf[:], in0=b_bf[:], in1=bd16[:],
                                         op=ALU.mult)
                    vE.inc(ii)
                block.vector(v2_)
                bc_v = vE.n

                def y2_(y):
                    y.wait_ge(vE.sem, bc_v)
                    ii = y.dma_start(out=scr_b[b, q], in_=bdel_bf[:])
                    dS.inc(ii)
                    ii = y.dma_start(out=scr_c[b, q], in_=c_bf[:])
                    dS.inc(ii)
                block.sync(y2_)
                return dS.n, dlb_d, halo_saved

            def scan(b, q, scr_done, dlb_d, halo_saved):
                zcnt = {}
                icnt = {}
                pair_free = {0: None, 1: None}
                bcast_d = {}
                ti = 0
                for nb in range(4):
                    def ybc_(y, nb=nb):
                        if nb == 0:
                            y.wait_ge(dS.sem, scr_done)
                            y.wait_ge(vE.sem, halo_saved)
                        else:
                            y.wait_ge(vE.sem, zcnt[nb * 32 - 1])
                        ii = y.dma_start(
                            out=bb_blk,
                            in_=scr_b[b, q, nb * 4:(nb + 1) * 4, :]
                            .rearrange("n t -> (n t)").partition_broadcast(P))
                        dS.inc(ii)
                        ii = y.dma_start(
                            out=cb_blk,
                            in_=scr_c[b, q, nb * 4:(nb + 1) * 4, :]
                            .rearrange("n t -> (n t)").partition_broadcast(P))
                        dS.inc(ii)
                    block.sync(ybc_)
                    bcast_d[nb] = dS.n

                    for g in range(8):
                        yp = (pp[3], pp[4]) if g % 2 == 0 else (pp[5], pp[6])
                        t0 = ti

                        def sda_(s, t0=t0, g=g, nb=nb):
                            for j in range(4):
                                i = t0 + j
                                col = g * 16 + nb * 4 + j
                                if i >= 2:
                                    s.wait_ge(vE.sem, zcnt.get(i - 2, 0) if i - 2 in zcnt
                                              else sda_pred[i - 2])
                                if i == 0:
                                    s.wait_ge(dS.sem, dlb_d)
                                ii = s.activation(out=da_v[i % 2], in_=dlb[:],
                                                  func=AF.Exp, scale=a_sb[:, col:col + 1])
                                sE.inc(ii)
                        sda_pred = {}
                        vbase = vE.n
                        for j in range(4):
                            sda_pred[t0 + j] = vbase + j + 1
                        block.scalar(sda_)
                        da_s = {t0 + j: sE.n - 3 + j for j in range(4)}

                        def vsc_(v, t0=t0, g=g, nb=nb):
                            for j in range(4):
                                i = t0 + j
                                col = g * 16 + nb * 4 + j
                                k = i % 2
                                if i >= 2:
                                    v.wait_ge(tE.sem, icnt.get(i - 2, 0) if i - 2 in icnt
                                              else vsc_pred[i - 2])
                                if j == 0:
                                    v.wait_ge(dS.sem, bcast_d[nb])
                                v.tensor_tensor(out=db_b[k][:], in0=x_conv[:, g, :],
                                                in1=bb_blk[:, j, :], op=ALU.mult)
                                v.wait_ge(sE.sem, da_s[i])
                                init = 0.0 if q == 0 else hcar[:, col:col + 1]
                                v.tensor_tensor_scan(out=h_b[k][:], data0=da_v[i % 2],
                                                     data1=db_b[k][:], initial=init,
                                                     op0=ALU.mult, op1=ALU.add)
                                if q == 0:
                                    v.tensor_copy(out=hcar[:, col:col + 1],
                                                  in_=h_b[k][:, HQ - 1:HQ])
                                ii = v.tensor_tensor(out=z_b[k][:], in0=h_b[k][:],
                                                     in1=cb_blk[:, j, :], op=ALU.mult)
                                vE.inc(ii)
                        vsc_pred = {}
                        tbase = tE.n
                        for j in range(4):
                            vsc_pred[t0 + j] = tbase + j + 1
                        block.vector(vsc_)
                        for j in range(4):
                            zcnt[t0 + j] = vbase + j + 1

                        def tid_(t, t0=t0, g=g, nb=nb, yp=yp):
                            pf = pair_free[g % 2]
                            if pf is not None:
                                t.wait_ge(pf[0].sem, pf[1])
                            for j in range(4):
                                i = t0 + j
                                k = i % 2
                                t.wait_ge(vE.sem, zcnt[i])
                                first = (j == 0 and nb == 0)
                                if j == 0 and nb > 0:
                                    t.matmul(yp[0][:, :], i128_sb[:], y_sb[:, g, 0:512],
                                             start=True, stop=False)
                                    t.matmul(yp[1][:, :], i128_sb[:], y_sb[:, g, 512:1024],
                                             start=True, stop=False)
                                last = (j == 3 and nb != 3)
                                t.matmul(yp[0][:, :], i128_sb[:], z_b[k][:, 0:512],
                                         start=first, stop=last)
                                ii = t.matmul(yp[1][:, :], i128_sb[:], z_b[k][:, 512:1024],
                                              start=first, stop=last)
                                if j == 3 and nb == 3:
                                    t.matmul(yp[0][:, :], dd_sb[:, g, :],
                                             x_conv[:, g, 0:512], start=False, stop=True)
                                    ii = t.matmul(yp[1][:, :], dd_sb[:, g, :],
                                                  x_conv[:, g, 512:1024],
                                                  start=False, stop=True)
                                tE.inc(ii)
                        block.tensor(tid_)
                        for j in range(4):
                            icnt[t0 + j] = tbase + j + 1
                        grp_t = tE.n

                        if nb < 3:
                            def se_(s, g=g, grp_t=grp_t, yp=yp):
                                s.wait_ge(tE.sem, grp_t)
                                s.copy(out=y_sb[:, g, 0:512], in_=yp[0][:, :])
                                ii = s.copy(out=y_sb[:, g, 512:1024], in_=yp[1][:, :])
                                sE.inc(ii)
                            block.scalar(se_)
                            pair_free[g % 2] = (sE, sE.n)
                        else:
                            def ve_(v, g=g, grp_t=grp_t, yp=yp):
                                v.wait_ge(tE.sem, grp_t)
                                v.tensor_tensor(out=t3[:, g, 0:512], in0=yp[0][:, :],
                                                in1=silu_z[:, g, 0:512], op=ALU.mult)
                                ii = v.tensor_tensor(out=t3[:, g, 512:1024], in0=yp[1][:, :],
                                                     in1=silu_z[:, g, 512:1024], op=ALU.mult)
                                vE.inc(ii)
                            block.vector(ve_)
                            pair_free[g % 2] = (vE, vE.n)
                        ti += 4
                return vE.n

            def wout_ln(b, q, t3_done):
                tok0 = q * HQ
                prev_resv = None
                for th in range(2):
                    tok = tok0 + th * TH

                    def y_(y, th=th, tok=tok):
                        y.wait_ge(dS.sem, dS.n)
                        ii = y.dma_start(
                            out=xres[:],
                            in_=x_fm[b].rearrange("(g p) t -> p g t", p=P)[:, :, tok:tok + TH])
                        dS.inc(ii)
                    block.sync(y_)
                    xres_d = dS.n

                    tE_w = tE.n
                    vE_w = vE.n

                    def gw_(g, th=th):
                        g.wait_ge(vE.sem, t3_done)
                        for ob in range(8):
                            if ob >= 2:
                                g.wait_ge(tE.sem, tE_w + ob - 1)
                            ii = g.dma_start(
                                out=wo_b[ob % 2][:],
                                in_=w_out.rearrange("(g p) o -> p g o", p=P)[:, :, ob * 128:(ob + 1) * 128])
                            dG.inc(ii)
                    block.gpsimd(gw_)
                    wo_d = [dG.n - 16 * (7 - ob) for ob in range(8)]

                    def t_(t, th=th, prev_resv=prev_resv):
                        t.wait_ge(vE.sem, t3_done)
                        if prev_resv is not None:
                            t.wait_ge(vE.sem, prev_resv)
                        for ob in range(8):
                            u = 7 if ob % 2 == 0 else 3
                            t.wait_ge(dG.sem, wo_d[ob])
                            if ob >= 2:
                                t.wait_ge(vE.sem, vE_w + ob - 1)
                            for g in range(8):
                                ii = t.matmul(pp[u][:, :], wo_b[ob % 2][:, g, :],
                                              t3[:, g, th * TH:(th + 1) * TH],
                                              start=(g == 0), stop=(g == 7))
                            tE.inc(ii)
                    block.tensor(t_)

                    def v_(v, th=th):
                        v.wait_ge(dS.sem, xres_d)
                        for ob in range(8):
                            u = 7 if ob % 2 == 0 else 3
                            v.wait_ge(tE.sem, tE_w + ob + 1)
                            ii = v.tensor_tensor(out=res[:, ob, :], in0=pp[u][:, :],
                                                 in1=xres[:, ob, :], op=ALU.add)
                            vE.inc(ii)
                    block.vector(v_)
                    res_v = vE.n
                    prev_resv = res_v

                    sE_l = sE.n
                    tE_l = tE.n

                    def s_(s, th=th):
                        for ob in range(8):
                            s.wait_ge(vE.sem, res_v - 8 + ob + 1)
                            if ob >= 2:
                                s.wait_ge(tE.sem, tE_l + ob - 1)
                            s.copy(out=rbf[:, ob % 2, :], in_=res[:, ob, :])
                            ii = s.activation(out=sbf[:, ob % 2, :], in_=res[:, ob, :],
                                              func=AF.Square)
                            sE.inc(ii)
                    block.scalar(s_)

                    def t2_(t, th=th):
                        for ob in range(8):
                            t.wait_ge(sE.sem, sE_l + ob + 1)
                            t.matmul(pp[0][0:1, :], ones_sb[:], rbf[:, ob % 2, :],
                                     start=(ob == 0), stop=(ob == 7), skip_group_check=True)
                            ii = t.matmul(pp[1][0:1, :], ones_sb[:], sbf[:, ob % 2, :],
                                          start=(ob == 0), stop=(ob == 7), skip_group_check=True)
                            tE.inc(ii)
                    block.tensor(t2_)
                    stat_t = tE.n

                    def s2_(s, th=th):
                        s.wait_ge(tE.sem, stat_t)
                        s.copy(out=mrow[:, 0:TH], in_=pp[0][0:1, :])
                        ii = s.activation(out=srow[:], in_=pp[0][0:1, :], func=AF.Square)
                        sE.inc(ii)
                    block.scalar(s2_)
                    m_s = sE.n

                    def v2_(v, th=th):
                        v.wait_ge(sE.sem, m_s)
                        ii = v.tensor_tensor(out=vrow[:], in0=pp[1][0:1, :], in1=srow[:],
                                             op=ALU.subtract)
                        vE.inc(ii)
                    block.vector(v2_)
                    var_v = vE.n

                    def s3_(s, th=th):
                        s.wait_ge(vE.sem, var_v)
                        ii = s.activation(out=srow[:], in_=vrow[:], func=AF.Sqrt, bias=eps_sb[0:1, 0:1])
                        sE.inc(ii)
                    block.scalar(s3_)
                    sd_s = sE.n

                    def v3_(v, th=th):
                        v.wait_ge(sE.sem, sd_s)
                        v.reciprocal(out=mrow[:, TH:2 * TH], in_=srow[:])
                        ii = v.tensor_tensor(out=mrow[:, 0:TH], in0=mrow[:, 0:TH],
                                             in1=mrow[:, TH:2 * TH], op=ALU.mult)
                        vE.inc(ii)
                    block.vector(v3_)
                    row_v = vE.n

                    def y2_(y, th=th):
                        y.wait_ge(vE.sem, row_v)
                        y.wait_ge(dS.sem, dS.n)
                        ii = y.dma_start(out=row_scr, in_=mrow[:])
                        dS.inc(ii)
                        y.wait_ge(dS.sem, dS.n)
                        dS.inc(y.dma_start(out=mrb[:],
                                    in_=row_scr[0:1, 0:TH].rearrange("a t -> (a t)")
                                    .partition_broadcast(P)))
                        dS.inc(y.dma_start(out=rsb[:],
                                    in_=row_scr[0:1, TH:2 * TH].rearrange("a t -> (a t)")
                                    .partition_broadcast(P)))
                    block.sync(y2_)
                    bc_d = dS.n

                    def v4_(v, th=th):
                        v.wait_ge(dS.sem, bc_d)
                        for ob in range(8):
                            v.tensor_tensor(out=d1t[:], in0=res[:, ob, :], in1=mrb[:],
                                            op=ALU.subtract)
                            ii = v.tensor_tensor(out=xres[:, ob, :], in0=d1t[:], in1=rsb[:],
                                                 op=ALU.mult)
                            vE.inc(ii)
                    block.vector(v4_)
                    xh_v = vE.n

                    def y3_(y, th=th, tok=tok):
                        y.wait_ge(vE.sem, xh_v)
                        ii = y.dma_start(
                            out=xhat_scr[b].rearrange("(g p) t -> p g t", p=P)[:, :, tok:tok + TH],
                            in_=xres[:])
                        dS.inc(ii)
                    block.sync(y3_)

            for b in range(NB):
                for q in range(2):
                    barrier()
                    scr_done, dlb_d, halo_saved = front(b, q)
                    t3_done = scan(b, q, scr_done, dlb_d, halo_saved)
                    wout_ln(b, q, t3_done)
            barrier()

        with contextlib.ExitStack() as s2s:
            e2 = s2s.enter_context
            w1_sb = e2(nc.sbuf_tensor([P, 8, DF], F8E5))
            w2_sb = e2(nc.sbuf_tensor([P, 32, D], F8E5))
            xh_b = [e2(nc.sbuf_tensor(f"xh{i}", [P, 8, S2], F32)) for i in range(2)]
            xh8 = e2(nc.sbuf_tensor([P, 8, S2], F8E5))
            hbf = e2(nc.sbuf_tensor([P, 32, S2], BF16))
            hf8 = e2(nc.sbuf_tensor([P, 32, S2], F8E5))
            y2b = e2(nc.sbuf_tensor([P, 8, S2], BF16))
            msk_b = [e2(nc.sbuf_tensor(f"msk{i}", [P, 8, S2], BF16)) for i in range(2)]
            dr2 = e2(nc.sbuf_tensor([P, 8, S2], F32))
            rs2 = e2(nc.sbuf_tensor([P, 8, S2], F32))
            rbf2 = e2(nc.sbuf_tensor([P, 2, S2], BF16))
            sbf2 = e2(nc.sbuf_tensor([P, 2, S2], BF16))
            mrow2 = e2(nc.sbuf_tensor([1, 2 * S2], F32))
            vrow2 = e2(nc.sbuf_tensor([1, S2], F32))
            srow2 = e2(nc.sbuf_tensor([1, S2], F32))
            mrb2 = e2(nc.sbuf_tensor([P, S2], F32))
            rsb2 = e2(nc.sbuf_tensor([P, S2], F32))
            d1t2 = e2(nc.sbuf_tensor([P, S2], F32))

            def g2_(g):
                dG.inc(g.dma_start(out=w1_sb[:],
                                   in_=w1t.rearrange("(g p) n -> p g n", p=P)))
                dG.inc(g.dma_start(out=w2_sb[:],
                                   in_=w2t.rearrange("(k p) o -> p k o", p=P)))
            block.gpsimd(g2_)
            w_d = dG.n

            for b in range(NB):
                for si in range(T // S2):
                    tok = si * S2
                    bi = si % 2
                    barrier()

                    def yl_(y, b=b, tok=tok, bi=bi):
                        dS.inc(y.dma_start(
                            out=xh_b[bi][:],
                            in_=xhat_scr[b].rearrange("(g p) t -> p g t", p=P)[:, :, tok:tok + S2]))
                        dS.inc(y.dma_start(
                            out=msk_b[bi][:],
                            in_=mask_fm[b].rearrange("(g p) t -> p g t", p=P)[:, :, tok:tok + S2]))
                    block.sync(yl_)
                    ld_d = dS.n

                    def vx_(v, bi=bi):
                        v.wait_ge(dS.sem, ld_d)
                        v.wait_ge(dG.sem, w_d)
                        ii = v.tensor_copy(
                            out=xh8[:].rearrange("p a t -> p (a t)"),
                            in_=xh_b[bi][:].rearrange("p a t -> p (a t)"))
                        vE.inc(ii)
                    block.vector(vx_)
                    x8_v = vE.n

                    tE_f1 = tE.n
                    sE_f1 = sE.n

                    def tf1_(t):
                        t.wait_ge(vE.sem, x8_v)
                        for nb2 in range(32):
                            if nb2 >= 2:
                                t.wait_ge(sE.sem, sE_f1 + nb2 - 1)
                            for g in range(8):
                                ii = t.matmul(pp[nb2 % 2][:, 0:S2],
                                              w1_sb[:, g, nb2 * 128:(nb2 + 1) * 128],
                                              xh8[:, g, :], start=(g == 0), stop=(g == 7))
                            tE.inc(ii)
                    block.tensor(tf1_)

                    def sf1_(s):
                        for nb2 in range(32):
                            s.wait_ge(tE.sem, tE_f1 + nb2 + 1)
                            ii = s.activation(out=hbf[:, nb2, :], in_=pp[nb2 % 2][:, 0:S2],
                                              func=AF.Relu, bias=b1q_sb[:, nb2:nb2 + 1],
                                              scale=1.0)
                            sE.inc(ii)
                    block.scalar(sf1_)
                    h_s = sE.n

                    def vh_(v):
                        v.wait_ge(sE.sem, h_s)
                        ii = v.tensor_copy(out=hf8[:].rearrange("p a t -> p (a t)"),
                                           in_=hbf[:].rearrange("p a t -> p (a t)"))
                        vE.inc(ii)
                    block.vector(vh_)
                    h8_v = vE.n

                    tE_f2 = tE.n
                    sE_f2 = sE.n

                    def tf2_(t):
                        t.wait_ge(vE.sem, h8_v)
                        for ob in range(8):
                            if ob >= 2:
                                t.wait_ge(sE.sem, sE_f2 + ob - 1)
                            for kb in range(32):
                                ii = t.matmul(pp[2 + ob % 2][:, 0:S2],
                                              w2_sb[:, kb, ob * 128:(ob + 1) * 128],
                                              hf8[:, kb, :], start=(kb == 0), stop=(kb == 31))
                            tE.inc(ii)
                    block.tensor(tf2_)

                    def sf2_(s):
                        for ob in range(8):
                            s.wait_ge(tE.sem, tE_f2 + ob + 1)
                            ii = s.activation(out=y2b[:, ob, :], in_=pp[2 + ob % 2][:, 0:S2],
                                              func=AF.Identity, bias=b2q_sb[:, ob:ob + 1],
                                              scale=1.0)
                            sE.inc(ii)
                    block.scalar(sf2_)
                    y2_s = sE.n

                    def vd_(v, bi=bi):
                        v.wait_ge(sE.sem, y2_s)
                        v.tensor_tensor(out=dr2[:].rearrange("p a t -> p (a t)"),
                                        in0=y2b[:].rearrange("p a t -> p (a t)"),
                                        in1=msk_b[bi][:].rearrange("p a t -> p (a t)"),
                                        op=ALU.mult)
                        ii = v.tensor_tensor(out=rs2[:].rearrange("p a t -> p (a t)"),
                                             in0=dr2[:].rearrange("p a t -> p (a t)"),
                                             in1=xh_b[bi][:].rearrange("p a t -> p (a t)"),
                                             op=ALU.add)
                        vE.inc(ii)
                    block.vector(vd_)
                    rs_v = vE.n

                    sE_l2 = sE.n
                    tE_l2 = tE.n

                    def sc2_(s):
                        s.wait_ge(vE.sem, rs_v)
                        for ob in range(8):
                            if ob >= 2:
                                s.wait_ge(tE.sem, tE_l2 + ob - 1)
                            s.copy(out=rbf2[:, ob % 2, :], in_=rs2[:, ob, :])
                            ii = s.activation(out=sbf2[:, ob % 2, :], in_=rs2[:, ob, :],
                                              func=AF.Square)
                            sE.inc(ii)
                    block.scalar(sc2_)

                    def ts2_(t):
                        for ob in range(8):
                            t.wait_ge(sE.sem, sE_l2 + ob + 1)
                            t.matmul(pp[4][0:1, 0:S2], ones_sb[:], rbf2[:, ob % 2, :],
                                     start=(ob == 0), stop=(ob == 7), skip_group_check=True)
                            ii = t.matmul(pp[5][0:1, 0:S2], ones_sb[:], sbf2[:, ob % 2, :],
                                          start=(ob == 0), stop=(ob == 7), skip_group_check=True)
                            tE.inc(ii)
                    block.tensor(ts2_)
                    st2 = tE.n

                    def sm2_(s):
                        s.wait_ge(tE.sem, st2)
                        s.copy(out=mrow2[:, 0:S2], in_=pp[4][0:1, 0:S2])
                        ii = s.activation(out=srow2[:], in_=pp[4][0:1, 0:S2], func=AF.Square)
                        sE.inc(ii)
                    block.scalar(sm2_)
                    m2_s = sE.n

                    def vv2_(v):
                        v.wait_ge(sE.sem, m2_s)
                        ii = v.tensor_tensor(out=vrow2[:], in0=pp[5][0:1, 0:S2], in1=srow2[:],
                                             op=ALU.subtract)
                        vE.inc(ii)
                    block.vector(vv2_)
                    var2 = vE.n

                    def ss2_(s):
                        s.wait_ge(vE.sem, var2)
                        ii = s.activation(out=srow2[:], in_=vrow2[:], func=AF.Sqrt, bias=eps_sb[0:1, 0:1])
                        sE.inc(ii)
                    block.scalar(ss2_)
                    sd2 = sE.n

                    def vr2_(v):
                        v.wait_ge(sE.sem, sd2)
                        v.reciprocal(out=mrow2[:, S2:2 * S2], in_=srow2[:])
                        ii = v.tensor_tensor(out=mrow2[:, 0:S2], in0=mrow2[:, 0:S2],
                                             in1=mrow2[:, S2:2 * S2], op=ALU.mult)
                        vE.inc(ii)
                    block.vector(vr2_)
                    row2 = vE.n

                    def yb2_(y):
                        y.wait_ge(vE.sem, row2)
                        y.wait_ge(dS.sem, dS.n)
                        ii = y.dma_start(out=row_scr2, in_=mrow2[:])
                        dS.inc(ii)
                        y.wait_ge(dS.sem, dS.n)
                        dS.inc(y.dma_start(out=mrb2[:],
                                    in_=row_scr2[0:1, 0:S2].rearrange("a t -> (a t)")
                                    .partition_broadcast(P)))
                        dS.inc(y.dma_start(out=rsb2[:],
                                    in_=row_scr2[0:1, S2:2 * S2].rearrange("a t -> (a t)")
                                    .partition_broadcast(P)))
                    block.sync(yb2_)
                    bb2 = dS.n

                    def vo2_(v):
                        v.wait_ge(dS.sem, bb2)
                        for ob in range(8):
                            v.tensor_tensor(out=d1t2[:], in0=rs2[:, ob, :], in1=mrb2[:],
                                            op=ALU.subtract)
                            ii = v.tensor_tensor(out=dr2[:, ob, :], in0=d1t2[:], in1=rsb2[:],
                                                 op=ALU.mult)
                            vE.inc(ii)
                    block.vector(vo2_)
                    o_v = vE.n

                    def yo2_(y, b=b, tok=tok):
                        y.wait_ge(vE.sem, o_v)
                        ii = y.dma_start(
                            out=out_fm[b].rearrange("(g p) t -> p g t", p=P)[:, :, tok:tok + S2],
                            in_=dr2[:])
                        dS.inc(ii)
                    block.sync(yo2_)

            def yfin_(y):
                y.wait_ge(dS.sem, dS.n)
                y.wait_ge(dG.sem, dG.n)
            block.sync(yfin_)

    return nc


_NC_CACHE = {}


def kernel(L0_attn_x, L0_attn_W_in, L0_attn_conv_w, L0_attn_conv_b,
           L0_attn_W_xproj, L0_attn_log_A, L0_attn_D_param, L0_attn_W_out,
           L0_ffn_W1, L0_ffn_b1, L0_ffn_W2, L0_ffn_b2):
    x = np.asarray(L0_attn_x, np.float32)
    w_in = np.ascontiguousarray(np.asarray(L0_attn_W_in, np.float32))
    conv_w = np.asarray(L0_attn_conv_w, np.float32)
    conv_b = np.asarray(L0_attn_conv_b, np.float32)
    w_xproj = np.asarray(L0_attn_W_xproj, np.float32)
    log_a = np.asarray(L0_attn_log_A, np.float32)
    d_param = np.asarray(L0_attn_D_param, np.float32)
    w_out = np.ascontiguousarray(np.asarray(L0_attn_W_out, np.float32))
    w1 = np.asarray(L0_ffn_W1, np.float32)
    b1 = np.asarray(L0_ffn_b1, np.float32)
    w2 = np.asarray(L0_ffn_W2, np.float32)
    b2 = np.asarray(L0_ffn_b2, np.float32)

    wxp_pad = np.zeros((D, 65), np.float32)
    wxp_pad[:, 0:16] = w_xproj[:, 0:16]
    wxp_pad[:, 32:48] = w_xproj[:, 16:32]
    wxp_pad[:, 64] = w_xproj[:, 32]
    w1t = np.ascontiguousarray(w1.T)
    w2t = np.ascontiguousarray(w2.T)
    loga_rr = np.ascontiguousarray(
        log_a.reshape(8, P, NST).transpose(1, 0, 2).reshape(P, 128))
    dgconv = np.zeros((8, 4, P, P), np.float32)
    for g in range(8):
        for j in range(4):
            np.fill_diagonal(dgconv[g, j], conv_w[g * P:(g + 1) * P, 0, j])
    dgd = np.zeros((8, P, P), np.float32)
    for g in range(8):
        np.fill_diagonal(dgd[g], d_param[g * P:(g + 1) * P])
    cbcol = np.ascontiguousarray(conv_b.reshape(8, P).T)
    b1rr = np.ascontiguousarray(b1.reshape(32, P).T)
    b2rr = np.ascontiguousarray(b2.reshape(8, P).T)
    i128a = np.eye(P, dtype=ml_dtypes.bfloat16)
    onesa = np.full((P, 1), 1.0 / 1024.0, dtype=ml_dtypes.bfloat16)

    import jax
    with jax.default_device(jax.devices("cpu")[0]):
        try:
            key42 = jax.random.key(42, impl="threefry2x32")
        except TypeError:
            key42 = jax.random.key(42)
        keep = np.asarray(jax.random.bernoulli(key42, 0.8, (16, T, D)))
    maskv = np.where(keep, np.float32(1.25), np.float32(0.0)).astype(ml_dtypes.bfloat16)

    if "nc" not in _NC_CACHE:
        _NC_CACHE["nc"] = build_program()
    nc = _NC_CACHE["nc"]

    in_maps = []
    for c in range(NCORE):
        xs = np.ascontiguousarray(x[2 * c:2 * c + 2].transpose(0, 2, 1))
        ms = np.ascontiguousarray(maskv[2 * c:2 * c + 2].transpose(0, 2, 1))
        in_maps.append(dict(
            x_fm=xs, w_in=w_in, w_out=w_out, wxp=wxp_pad, w1t=w1t, w2t=w2t,
            loga_r=loga_rr, dgconv=dgconv, dgd=dgd, cbcol=cbcol, b1r=b1rr,
            b2r=b2rr, mask_fm=ms, i128=i128a, onesln=onesa,
        ))

    res = run_bass_kernel_spmd(nc, in_maps, core_ids=list(range(NCORE)))
    outs = [np.asarray(res.results[c]["out_fm"]).transpose(0, 2, 1)
            for c in range(NCORE)]
    return np.ascontiguousarray(np.concatenate(outs, axis=0))


# version 04
# speedup vs baseline: 1.1569x; 1.1569x over previous
import contextlib

import numpy as np
import ml_dtypes

import concourse.bass as bass
import concourse.mybir as mybir
from concourse.bass_utils import run_bass_kernel_spmd

F32 = mybir.dt.float32
BF16 = mybir.dt.bfloat16
F8E5 = mybir.dt.float8e5
AF = mybir.ActivationFunctionType
ALU = mybir.AluOpType

P = 128
D = 1024
T = 2048
NB = 2
NST = 16
HQ = 1024
TH = 512
S2 = 512
DF = 4096
NCORE = 8
EPS = 1e-5


class Ctr:
    def __init__(self, sem, step):
        self.sem = sem
        self.n = 0
        self.step = step

    def inc(self, instr):
        instr.then_inc(self.sem, self.step)
        self.n += self.step
        return self.n


def build_program():
    nc = bass.Bass("TRN2", target_bir_lowering=False, debug=False)
    dt = nc.dram_tensor

    x_fm = dt("x_fm", [NB, D, T], F32, kind="ExternalInput").ap()
    w_in = dt("w_in", [D, 2 * D], F32, kind="ExternalInput").ap()
    w_out = dt("w_out", [D, D], F32, kind="ExternalInput").ap()
    wxp = dt("wxp", [D, 65], F32, kind="ExternalInput").ap()
    w1t = dt("w1t", [D, DF], F32, kind="ExternalInput").ap()
    w2t = dt("w2t", [DF, D], F32, kind="ExternalInput").ap()
    loga_r = dt("loga_r", [P, 128], F32, kind="ExternalInput").ap()
    dgconv = dt("dgconv", [8, 4, P, P], F32, kind="ExternalInput").ap()
    dgd = dt("dgd", [8, P, P], F32, kind="ExternalInput").ap()
    cbcol = dt("cbcol", [P, 8], F32, kind="ExternalInput").ap()
    b1r = dt("b1r", [P, 32], F32, kind="ExternalInput").ap()
    b2r = dt("b2r", [P, 8], F32, kind="ExternalInput").ap()
    mask_fm = dt("mask_fm", [NB, D, T], BF16, kind="ExternalInput").ap()
    i128 = dt("i128", [P, P], BF16, kind="ExternalInput").ap()
    onesln = dt("onesln", [P, 1], BF16, kind="ExternalInput").ap()

    out_fm = dt("out_fm", [NB, D, T], F32, kind="ExternalOutput").ap()
    xhat_scr = dt("xhat_scr", [NB, D, T], F32, kind="Internal").ap()
    scr_b = dt("scr_b", [NB, 2, NST, HQ], BF16, kind="Internal").ap()
    scr_c = dt("scr_c", [NB, 2, NST, HQ], BF16, kind="Internal").ap()
    scr_d = dt("scr_d", [NB, 2, 1, HQ], F32, kind="Internal").ap()
    row_scr = dt("row_scr", [1, 2 * TH], F32, kind="Internal").ap()
    row_scr2 = dt("row_scr2", [1, 2 * S2], F32, kind="Internal").ap()

    st = contextlib.ExitStack()
    with st:
        e = st.enter_context
        dS = Ctr(e(nc.semaphore("dS")), 16)
        dG = Ctr(e(nc.semaphore("dG")), 16)
        sE = Ctr(e(nc.semaphore("sE")), 1)
        vE = Ctr(e(nc.semaphore("vE")), 1)
        tE = Ctr(e(nc.semaphore("tE")), 1)
        block = e(nc.Block())

        def barrier():
            waits = [(dS.sem, dS.n), (dG.sem, dG.n), (sE.sem, sE.n),
                     (vE.sem, vE.n), (tE.sem, tE.n)]

            def emit(eng):
                for sem, n in waits:
                    eng.wait_ge(sem, n)
            block.sync(emit)
            block.gpsimd(emit)
            block.scalar(emit)
            block.vector(emit)
            block.tensor(emit)

        a_sb = e(nc.sbuf_tensor([P, 128], F32))
        b1q_sb = e(nc.sbuf_tensor([P, 32], F32))
        b2q_sb = e(nc.sbuf_tensor([P, 8], F32))
        b1f8 = e(nc.sbuf_tensor([P, 32], F8E5))
        b2f8 = e(nc.sbuf_tensor([P, 8], F8E5))
        i128_sb = e(nc.sbuf_tensor([P, P], BF16))
        ones_sb = e(nc.sbuf_tensor([P, 1], BF16))
        eps_sb = e(nc.sbuf_tensor([P, 1], F32))
        negh_sb = e(nc.sbuf_tensor([P, 1], F32))
        pp = [e(nc.psum_tensor(f"pp{i}", [P, 512], F32)) for i in range(8)]

        with contextlib.ExitStack() as s1:
            e1 = s1.enter_context
            dg_sb = e1(nc.sbuf_tensor([P, 8, 4, P], BF16))
            dd_sb = e1(nc.sbuf_tensor([P, 8, P], BF16))
            wxp_sb = e1(nc.sbuf_tensor([P, 8, 65], BF16))
            cb_sb = e1(nc.sbuf_tensor([P, 8], F32))
            loga_sb = e1(nc.sbuf_tensor([P, 128], F32))

            x_bf = e1(nc.sbuf_tensor([P, 8, HQ], BF16))
            ubuf = e1(nc.sbuf_tensor([P, 16 * HQ], BF16))
            halo = e1(nc.sbuf_tensor([P, 8, 3], BF16))
            x_conv = e1(nc.sbuf_tensor([P, 8, HQ], BF16))
            silu_z = e1(nc.sbuf_tensor([P, 8, HQ], BF16))
            dlb = e1(nc.sbuf_tensor([P, HQ], F32))
            b_bf = e1(nc.sbuf_tensor([NST, HQ], BF16))
            bd16 = e1(nc.sbuf_tensor([NST, HQ], F32))
            bdel_bf = e1(nc.sbuf_tensor([NST, HQ], BF16))
            c_bf = e1(nc.sbuf_tensor([NST, HQ], BF16))
            erow = e1(nc.sbuf_tensor([1, HQ], F32))
            drow = e1(nc.sbuf_tensor([1, HQ], F32))
            uf32 = e1(nc.sbuf_tensor([P, 8 * TH], F32))
            db_b = [e1(nc.sbuf_tensor(f"db{i}", [P, HQ], BF16)) for i in range(2)]
            h_b = [e1(nc.sbuf_tensor(f"hb{i}", [P, HQ], BF16)) for i in range(2)]
            z_b = [e1(nc.sbuf_tensor(f"zb{i}", [P, HQ], BF16)) for i in range(2)]
            hcar = e1(nc.sbuf_tensor([P, 128], F32))
            t3 = e1(nc.sbuf_tensor([P, 8, HQ], BF16))
            win_b = [e1(nc.sbuf_tensor(f"win{i}", [P, 8, 128], BF16)) for i in range(2)]
            wo_b = [e1(nc.sbuf_tensor(f"wo{i}", [P, 8, 128], BF16)) for i in range(2)]
            xres = e1(nc.sbuf_tensor([P, 8, TH], F32))
            d1t = e1(nc.sbuf_tensor([P, TH], F32))
            rbf = e1(nc.sbuf_tensor([P, 2, TH], BF16))
            sbf = e1(nc.sbuf_tensor([P, 2, TH], BF16))
            mrow = e1(nc.sbuf_tensor([1, 2 * TH], F32))
            vrow = e1(nc.sbuf_tensor([1, TH], F32))
            srow = e1(nc.sbuf_tensor([1, TH], F32))
            mrb = e1(nc.sbuf_tensor([P, TH], F32))
            rsb = e1(nc.sbuf_tensor([P, TH], F32))

            xm = ubuf[:, 0:8 * (HQ + 3)].rearrange("p (g t) -> p g t", g=8)
            bbv = [ubuf[:, s * 4 * HQ:(s + 1) * 4 * HQ]
                   .rearrange("p (n t) -> p n t", n=4) for s in range(2)]
            cbv = [ubuf[:, (8 + 4 * s) * HQ:(12 + 4 * s) * HQ]
                   .rearrange("p (n t) -> p n t", n=4) for s in range(2)]
            res = uf32[:].rearrange("p (g t) -> p g t", g=8)
            da_v = [uf32[:, 0:HQ], uf32[:, HQ:2 * HQ]]

            def g_(g):
                dG.inc(g.dma_start(out=dg_sb[:],
                                   in_=dgconv.rearrange("g j k p -> k g j p")))
                dG.inc(g.dma_start(out=dd_sb[:],
                                   in_=dgd.rearrange("g k p -> k g p")))
                dG.inc(g.dma_start(out=wxp_sb[:],
                                   in_=wxp.rearrange("(g p) o -> p g o", p=P)))
                dG.inc(g.dma_start(out=b1f8[:], in_=b1r))
                dG.inc(g.dma_start(out=b2f8[:], in_=b2r))
            block.gpsimd(g_)

            def y0_(y):
                dS.inc(y.dma_start(out=loga_sb[:], in_=loga_r))
                dS.inc(y.dma_start(out=cb_sb[:], in_=cbcol))
                dS.inc(y.dma_start(out=i128_sb[:], in_=i128))
                dS.inc(y.dma_start(out=ones_sb[:], in_=onesln))
            block.sync(y0_)

            def s0_(s):
                s.wait_ge(dS.sem, dS.n)
                s.wait_ge(dG.sem, dG.n)
                s.activation(out=a_sb[:], in_=loga_sb[:], func=AF.Exp)
                s.copy(out=b1q_sb[:], in_=b1f8[:])
                ii = s.copy(out=b2q_sb[:], in_=b2f8[:])
                sE.inc(ii)
            block.scalar(s0_)

            def v0_(v):
                v.wait_ge(sE.sem, sE.n)
                v.memset(eps_sb[:], EPS)
                v.memset(negh_sb[:], -0.5)
                ii = v.tensor_scalar_mul(out=a_sb[:], in0=a_sb[:], scalar1=-1.0)
                vE.inc(ii)
            block.vector(v0_)

            def front(b, q):
                tok0 = q * HQ
                tE_base = tE.n
                sE_base = sE.n

                def g_(g):
                    ii = g.dma_start(
                        out=x_bf[:],
                        in_=x_fm[b].rearrange("(g p) t -> p g t", p=P)[:, :, tok0:tok0 + HQ])
                    dG.inc(ii)
                    for ob in range(16):
                        if ob >= 2:
                            g.wait_ge(tE.sem, tE_base + 2 * (ob - 2) + 2)
                        ii = g.dma_start(
                            out=win_b[ob % 2][:],
                            in_=w_in.rearrange("(g p) o -> p g o", p=P)[:, :, ob * 128:(ob + 1) * 128])
                        dG.inc(ii)
                block.gpsimd(g_)
                x_d = dG.n - 16 * 16
                win_d = [x_d + 16 * (ob + 1) for ob in range(16)]

                def v_(v):
                    if q == 0:
                        ii = v.memset(xm[:, :, 0:3], 0.0)
                    else:
                        ii = v.tensor_copy(out=xm[:, :, 0:3], in_=halo[:])
                    vE.inc(ii)
                block.vector(v_)
                halo_v = vE.n

                def t_(t):
                    t.wait_ge(dG.sem, x_d)
                    t.wait_ge(vE.sem, halo_v)
                    for ob in range(16):
                        t.wait_ge(dG.sem, win_d[ob])
                        for th in range(2):
                            k = 2 * ob + th
                            if k >= 2:
                                t.wait_ge(sE.sem, sE_base + k - 1)
                            for g in range(8):
                                ii = t.matmul(pp[k % 2][:, :], win_b[ob % 2][:, g, :],
                                              x_bf[:, g, th * 512:(th + 1) * 512],
                                              start=(g == 0), stop=(g == 7))
                            tE.inc(ii)
                block.tensor(t_)

                def s_(s):
                    s.wait_ge(vE.sem, halo_v)
                    for ob in range(16):
                        for th in range(2):
                            k = 2 * ob + th
                            s.wait_ge(tE.sem, tE_base + k + 1)
                            if ob < 8:
                                ii = s.copy(out=xm[:, ob, 3 + th * 512: 3 + (th + 1) * 512],
                                            in_=pp[k % 2][:, :])
                            else:
                                ii = s.activation(out=silu_z[:, ob - 8, th * 512:(th + 1) * 512],
                                                  in_=pp[k % 2][:, :], func=AF.Silu)
                            sE.inc(ii)
                block.scalar(s_)
                xz_evac = sE.n

                def v1_(v):
                    v.wait_ge(sE.sem, xz_evac)
                    ii = v.tensor_copy(out=halo[:], in_=xm[:, :, HQ:HQ + 3])
                    vE.inc(ii)
                block.vector(v1_)
                halo_saved = vE.n

                tE_c = tE.n
                sE_c = sE.n

                def t2_(t):
                    t.wait_ge(sE.sem, xz_evac)
                    for g in range(8):
                        for th in range(2):
                            k = 2 * g + th
                            if k >= 2:
                                t.wait_ge(sE.sem, sE_c + k - 1)
                            for j in range(4):
                                ii = t.matmul(pp[k % 2][:, :], dg_sb[:, g, j, :],
                                              xm[:, g, th * 512 + j: th * 512 + j + 512],
                                              start=(j == 0), stop=(j == 3))
                            tE.inc(ii)
                block.tensor(t2_)

                def s2_(s):
                    for g in range(8):
                        for th in range(2):
                            k = 2 * g + th
                            s.wait_ge(tE.sem, tE_c + k + 1)
                            ii = s.activation(out=x_conv[:, g, th * 512:(th + 1) * 512],
                                              in_=pp[k % 2][:, :], func=AF.Silu,
                                              bias=cb_sb[:, g:g + 1], scale=1.0)
                            sE.inc(ii)
                block.scalar(s2_)
                conv_evac = sE.n

                proj_t = {}
                proj_s = {}
                for th in range(2):
                    def tp_(t, th=th):
                        if th == 0:
                            t.wait_ge(sE.sem, conv_evac)
                        else:
                            t.wait_ge(sE.sem, proj_s[0])
                        for g in range(8):
                            ii = t.matmul(pp[2][0:65, :], wxp_sb[:, g, :],
                                          x_conv[:, g, th * 512:(th + 1) * 512],
                                          start=(g == 0), stop=(g == 7))
                        tE.inc(ii)
                    block.tensor(tp_)
                    proj_t[th] = tE.n

                    def sp_(s, th=th):
                        s.wait_ge(tE.sem, proj_t[th])
                        sl = slice(th * 512, (th + 1) * 512)
                        s.copy(out=b_bf[:, sl], in_=pp[2][0:16, :])
                        s.copy(out=c_bf[:, sl], in_=pp[2][32:48, :])
                        ii = s.activation(out=erow[:, sl], in_=pp[2][64:65, :], func=AF.Exp)
                        sE.inc(ii)
                    block.scalar(sp_)
                    proj_s[th] = sE.n

                def s3_(s):
                    ii = s.activation(out=drow[:], in_=erow[:], func=AF.Ln, bias=1.0)
                    sE.inc(ii)
                block.scalar(s3_)
                drow_s = sE.n

                def y_(y):
                    y.wait_ge(sE.sem, drow_s)
                    ii = y.dma_start(out=scr_d[b, q], in_=drow[:])
                    dS.inc(ii)
                    y.wait_ge(dS.sem, dS.n)
                    ii = y.dma_start(
                        out=dlb[:],
                        in_=scr_d[b, q].rearrange("a t -> (a t)").partition_broadcast(P))
                    dS.inc(ii)
                    ii = y.dma_start(
                        out=bd16[:],
                        in_=scr_d[b, q].rearrange("a t -> (a t)").partition_broadcast(NST))
                    dS.inc(ii)
                block.sync(y_)
                dlb_d = dS.n - 16
                bd16_d = dS.n

                def v2_(v):
                    v.wait_ge(dS.sem, bd16_d)
                    ii = v.tensor_tensor(out=bdel_bf[:], in0=b_bf[:], in1=bd16[:],
                                         op=ALU.mult)
                    vE.inc(ii)
                block.vector(v2_)
                bc_v = vE.n

                def y2_(y):
                    y.wait_ge(vE.sem, bc_v)
                    ii = y.dma_start(out=scr_b[b, q], in_=bdel_bf[:])
                    dS.inc(ii)
                    ii = y.dma_start(out=scr_c[b, q], in_=c_bf[:])
                    dS.inc(ii)
                block.sync(y2_)
                return dS.n, dlb_d, halo_saved

            def scan(b, q, scr_done, dlb_d, halo_saved):
                pair_free = {0: None, 1: None}
                slot_user = {0: None, 1: None}
                zcnt = {}
                icnt = {}
                ti = 0
                for g in range(8):
                    yp = (pp[3], pp[4]) if g % 2 == 0 else (pp[5], pp[6])
                    for nb in range(4):
                        slot = nb % 2
                        su = slot_user[slot]

                        def gbc_(gq, b=b, q=q, nb=nb, slot=slot, su=su, first=(g == 0 and nb <= 1)):
                            if su is not None:
                                gq.wait_ge(vE.sem, su)
                            if first:
                                gq.wait_ge(dS.sem, scr_done)
                                gq.wait_ge(vE.sem, halo_saved)
                            dG.inc(gq.dma_start(
                                out=bbv[slot],
                                in_=scr_b[b, q, nb * 4:(nb + 1) * 4, :]
                                .rearrange("n t -> (n t)").partition_broadcast(P)))
                        block.gpsimd(gbc_)
                        bb_d = dG.n

                        def ybc_(y, b=b, q=q, nb=nb, slot=slot, su=su, first=(g == 0 and nb <= 1)):
                            if su is not None:
                                y.wait_ge(vE.sem, su)
                            if first:
                                y.wait_ge(dS.sem, scr_done)
                                y.wait_ge(vE.sem, halo_saved)
                            dS.inc(y.dma_start(
                                out=cbv[slot],
                                in_=scr_c[b, q, nb * 4:(nb + 1) * 4, :]
                                .rearrange("n t -> (n t)").partition_broadcast(P)))
                        block.sync(ybc_)
                        cb_d = dS.n

                        t0 = ti
                        vbase = vE.n
                        sda_pred = {t0 + j: vbase + j + 1 for j in range(4)}

                        def sda_(s, t0=t0, g=g, nb=nb, sda_pred=sda_pred):
                            for j in range(4):
                                i = t0 + j
                                col = g * 16 + nb * 4 + j
                                if i >= 2:
                                    s.wait_ge(vE.sem, zcnt[i - 2] if i - 2 in zcnt
                                              else sda_pred[i - 2])
                                if i == 0:
                                    s.wait_ge(dS.sem, dlb_d)
                                ii = s.activation(out=da_v[i % 2], in_=dlb[:],
                                                  func=AF.Exp, scale=a_sb[:, col:col + 1])
                                sE.inc(ii)
                        block.scalar(sda_)
                        da_s = {t0 + j: sE.n - 3 + j for j in range(4)}

                        tbase = tE.n
                        vsc_pred = {t0 + j: tbase + j + 1 for j in range(4)}

                        def vsc_(v, t0=t0, g=g, nb=nb, slot=slot, bb_d=bb_d, cb_d=cb_d,
                                 da_s=da_s, vsc_pred=vsc_pred):
                            for j in range(4):
                                i = t0 + j
                                col = g * 16 + nb * 4 + j
                                k = i % 2
                                if i >= 2:
                                    v.wait_ge(tE.sem, icnt[i - 2] if i - 2 in icnt
                                              else vsc_pred[i - 2])
                                if j == 0:
                                    v.wait_ge(dG.sem, bb_d)
                                    v.wait_ge(dS.sem, cb_d)
                                v.tensor_tensor(out=db_b[k][:], in0=x_conv[:, g, :],
                                                in1=bbv[slot][:, j, :], op=ALU.mult)
                                v.wait_ge(sE.sem, da_s[i])
                                init = 0.0 if q == 0 else hcar[:, col:col + 1]
                                v.tensor_tensor_scan(out=h_b[k][:], data0=da_v[i % 2],
                                                     data1=db_b[k][:], initial=init,
                                                     op0=ALU.mult, op1=ALU.add)
                                if q == 0:
                                    v.tensor_copy(out=hcar[:, col:col + 1],
                                                  in_=h_b[k][:, HQ - 1:HQ])
                                ii = v.tensor_tensor(out=z_b[k][:], in0=h_b[k][:],
                                                     in1=cbv[slot][:, j, :], op=ALU.mult)
                                vE.inc(ii)
                        block.vector(vsc_)
                        for j in range(4):
                            zcnt[t0 + j] = vbase + j + 1
                        slot_user[slot] = vE.n

                        def tid_(t, t0=t0, g=g, nb=nb, yp=yp):
                            if nb == 0:
                                pf = pair_free[g % 2]
                                if pf is not None:
                                    t.wait_ge(pf[0].sem, pf[1])
                            for j in range(4):
                                i = t0 + j
                                k = i % 2
                                t.wait_ge(vE.sem, zcnt[i])
                                first = (j == 0 and nb == 0)
                                t.matmul(yp[0][:, :], i128_sb[:], z_b[k][:, 0:512],
                                         start=first, stop=False)
                                ii = t.matmul(yp[1][:, :], i128_sb[:], z_b[k][:, 512:1024],
                                              start=first, stop=False)
                                if j == 3 and nb == 3:
                                    t.matmul(yp[0][:, :], dd_sb[:, g, :],
                                             x_conv[:, g, 0:512], start=False, stop=True)
                                    ii = t.matmul(yp[1][:, :], dd_sb[:, g, :],
                                                  x_conv[:, g, 512:1024],
                                                  start=False, stop=True)
                                tE.inc(ii)
                        block.tensor(tid_)
                        for j in range(4):
                            icnt[t0 + j] = tbase + j + 1
                        ti += 4
                    grp_t = tE.n

                    def ve_(v, g=g, grp_t=grp_t, yp=yp):
                        v.wait_ge(tE.sem, grp_t)
                        v.tensor_tensor(out=t3[:, g, 0:512], in0=yp[0][:, :],
                                        in1=silu_z[:, g, 0:512], op=ALU.mult)
                        ii = v.tensor_tensor(out=t3[:, g, 512:1024], in0=yp[1][:, :],
                                             in1=silu_z[:, g, 512:1024], op=ALU.mult)
                        vE.inc(ii)
                    block.vector(ve_)
                    pair_free[g % 2] = (vE, vE.n)
                return vE.n

            def wout_ln(b, q, t3_done):
                tok0 = q * HQ
                prev_resv = None
                for th in range(2):
                    tok = tok0 + th * TH

                    def y_(y, th=th, tok=tok):
                        y.wait_ge(dS.sem, dS.n)
                        ii = y.dma_start(
                            out=xres[:],
                            in_=x_fm[b].rearrange("(g p) t -> p g t", p=P)[:, :, tok:tok + TH])
                        dS.inc(ii)
                    block.sync(y_)
                    xres_d = dS.n

                    tE_w = tE.n
                    vE_w = vE.n

                    def gw_(g, th=th):
                        g.wait_ge(vE.sem, t3_done)
                        for ob in range(8):
                            if ob >= 2:
                                g.wait_ge(tE.sem, tE_w + ob - 1)
                            ii = g.dma_start(
                                out=wo_b[ob % 2][:],
                                in_=w_out.rearrange("(g p) o -> p g o", p=P)[:, :, ob * 128:(ob + 1) * 128])
                            dG.inc(ii)
                    block.gpsimd(gw_)
                    wo_d = [dG.n - 16 * (7 - ob) for ob in range(8)]

                    def t_(t, th=th, prev_resv=prev_resv):
                        t.wait_ge(vE.sem, t3_done)
                        if prev_resv is not None:
                            t.wait_ge(vE.sem, prev_resv)
                        for ob in range(8):
                            u = 7 if ob % 2 == 0 else 3
                            t.wait_ge(dG.sem, wo_d[ob])
                            if ob >= 2:
                                t.wait_ge(vE.sem, vE_w + ob - 1)
                            for g in range(8):
                                ii = t.matmul(pp[u][:, :], wo_b[ob % 2][:, g, :],
                                              t3[:, g, th * TH:(th + 1) * TH],
                                              start=(g == 0), stop=(g == 7))
                            tE.inc(ii)
                    block.tensor(t_)

                    def v_(v, th=th):
                        v.wait_ge(dS.sem, xres_d)
                        for ob in range(8):
                            u = 7 if ob % 2 == 0 else 3
                            v.wait_ge(tE.sem, tE_w + ob + 1)
                            ii = v.tensor_tensor(out=res[:, ob, :], in0=pp[u][:, :],
                                                 in1=xres[:, ob, :], op=ALU.add)
                            vE.inc(ii)
                    block.vector(v_)
                    res_v = vE.n
                    prev_resv = res_v

                    sE_l = sE.n
                    tE_l = tE.n

                    def s_(s, th=th):
                        for ob in range(8):
                            s.wait_ge(vE.sem, res_v - 8 + ob + 1)
                            if ob >= 2:
                                s.wait_ge(tE.sem, tE_l + ob - 1)
                            s.copy(out=rbf[:, ob % 2, :], in_=res[:, ob, :])
                            ii = s.activation(out=sbf[:, ob % 2, :], in_=res[:, ob, :],
                                              func=AF.Square)
                            sE.inc(ii)
                    block.scalar(s_)

                    def t2_(t, th=th):
                        for ob in range(8):
                            t.wait_ge(sE.sem, sE_l + ob + 1)
                            t.matmul(pp[0][0:1, :], ones_sb[:], rbf[:, ob % 2, :],
                                     start=(ob == 0), stop=(ob == 7), skip_group_check=True)
                            ii = t.matmul(pp[1][0:1, :], ones_sb[:], sbf[:, ob % 2, :],
                                          start=(ob == 0), stop=(ob == 7), skip_group_check=True)
                            tE.inc(ii)
                    block.tensor(t2_)
                    stat_t = tE.n

                    def s2_(s, th=th):
                        s.wait_ge(tE.sem, stat_t)
                        s.copy(out=mrow[:, 0:TH], in_=pp[0][0:1, :])
                        ii = s.activation(out=srow[:], in_=pp[0][0:1, :], func=AF.Square)
                        sE.inc(ii)
                    block.scalar(s2_)
                    m_s = sE.n

                    def v2_(v, th=th):
                        v.wait_ge(sE.sem, m_s)
                        ii = v.tensor_tensor(out=vrow[:], in0=pp[1][0:1, :], in1=srow[:],
                                             op=ALU.subtract)
                        vE.inc(ii)
                    block.vector(v2_)
                    var_v = vE.n

                    def s3_(s, th=th):
                        s.wait_ge(vE.sem, var_v)
                        ii = s.activation(out=srow[:], in_=vrow[:], func=AF.Sqrt, bias=eps_sb[0:1, 0:1])
                        sE.inc(ii)
                    block.scalar(s3_)
                    sd_s = sE.n

                    def v3_(v, th=th):
                        v.wait_ge(sE.sem, sd_s)
                        v.reciprocal(out=mrow[:, TH:2 * TH], in_=srow[:])
                        ii = v.tensor_tensor(out=mrow[:, 0:TH], in0=mrow[:, 0:TH],
                                             in1=mrow[:, TH:2 * TH], op=ALU.mult)
                        vE.inc(ii)
                    block.vector(v3_)
                    row_v = vE.n

                    def y2_(y, th=th):
                        y.wait_ge(vE.sem, row_v)
                        y.wait_ge(dS.sem, dS.n)
                        ii = y.dma_start(out=row_scr, in_=mrow[:])
                        dS.inc(ii)
                        y.wait_ge(dS.sem, dS.n)
                        dS.inc(y.dma_start(out=mrb[:],
                                    in_=row_scr[0:1, 0:TH].rearrange("a t -> (a t)")
                                    .partition_broadcast(P)))
                        dS.inc(y.dma_start(out=rsb[:],
                                    in_=row_scr[0:1, TH:2 * TH].rearrange("a t -> (a t)")
                                    .partition_broadcast(P)))
                    block.sync(y2_)
                    bc_d = dS.n

                    def v4_(v, th=th):
                        v.wait_ge(dS.sem, bc_d)
                        for ob in range(8):
                            v.tensor_tensor(out=d1t[:], in0=res[:, ob, :], in1=mrb[:],
                                            op=ALU.subtract)
                            ii = v.tensor_tensor(out=xres[:, ob, :], in0=d1t[:], in1=rsb[:],
                                                 op=ALU.mult)
                            vE.inc(ii)
                    block.vector(v4_)
                    xh_v = vE.n

                    def y3_(y, th=th, tok=tok):
                        y.wait_ge(vE.sem, xh_v)
                        ii = y.dma_start(
                            out=xhat_scr[b].rearrange("(g p) t -> p g t", p=P)[:, :, tok:tok + TH],
                            in_=xres[:])
                        dS.inc(ii)
                    block.sync(y3_)

            for b in range(NB):
                for q in range(2):
                    barrier()
                    scr_done, dlb_d, halo_saved = front(b, q)
                    t3_done = scan(b, q, scr_done, dlb_d, halo_saved)
                    wout_ln(b, q, t3_done)
            barrier()

        with contextlib.ExitStack() as s2s:
            e2 = s2s.enter_context
            w1_sb = e2(nc.sbuf_tensor([P, 8, DF], F8E5))
            w2_sb = e2(nc.sbuf_tensor([P, 32, D], F8E5))
            xh_b = [e2(nc.sbuf_tensor(f"xh{i}", [P, 8, S2], F32)) for i in range(2)]
            xh8 = e2(nc.sbuf_tensor([P, 8, S2], F8E5))
            hbf = e2(nc.sbuf_tensor([P, 16, S2], BF16))
            hf8 = e2(nc.sbuf_tensor([P, 32, S2], F8E5))
            y2b = e2(nc.sbuf_tensor([P, 8, S2], BF16))
            msk_b = [e2(nc.sbuf_tensor(f"msk{i}", [P, 8, S2], BF16)) for i in range(2)]
            dr2 = e2(nc.sbuf_tensor([P, 8, S2], F32))
            rs2 = e2(nc.sbuf_tensor([P, 8, S2], F32))
            rbf2 = e2(nc.sbuf_tensor([P, 2, S2], BF16))
            sbf2 = e2(nc.sbuf_tensor([P, 2, S2], BF16))
            mrow2 = e2(nc.sbuf_tensor([1, 2 * S2], F32))
            srow2 = e2(nc.sbuf_tensor([1, S2], F32))
            mrb2 = e2(nc.sbuf_tensor([P, S2], F32))
            rsb2 = e2(nc.sbuf_tensor([P, S2], F32))

            def g2_(g):
                dG.inc(g.dma_start(out=w1_sb[:],
                                   in_=w1t.rearrange("(g p) n -> p g n", p=P)))
                dG.inc(g.dma_start(out=w2_sb[:],
                                   in_=w2t.rearrange("(k p) o -> p k o", p=P)))
            block.gpsimd(g2_)
            w_d = dG.n

            pv = dict(vd=None, f1mm=None, vh=None, f2mm=None, st2=None,
                      out_d=None, rows_d=None, o_v=None)

            slices = [(b, si) for b in range(NB) for si in range(T // S2)]
            for gi, (b, si) in enumerate(slices):
                tok = si * S2
                bi = gi % 2

                def yl_(y, b=b, tok=tok, bi=bi, gi=gi):
                    if pv["vd"] is not None and gi >= 2:
                        y.wait_ge(vE.sem, pv[("vd", gi - 2)])
                    dS.inc(y.dma_start(
                        out=xh_b[bi][:],
                        in_=xhat_scr[b].rearrange("(g p) t -> p g t", p=P)[:, :, tok:tok + S2]))
                    dS.inc(y.dma_start(
                        out=msk_b[bi][:],
                        in_=mask_fm[b].rearrange("(g p) t -> p g t", p=P)[:, :, tok:tok + S2]))
                block.sync(yl_)
                ld_d = dS.n

                def vx_(v, bi=bi, ld_d=ld_d, gi=gi):
                    v.wait_ge(dS.sem, ld_d)
                    if gi == 0:
                        v.wait_ge(dG.sem, w_d)
                    if pv["f1mm"] is not None:
                        v.wait_ge(tE.sem, pv["f1mm"])
                    ii = v.tensor_copy(
                        out=xh8[:].rearrange("p a t -> p (a t)"),
                        in_=xh_b[bi][:].rearrange("p a t -> p (a t)"))
                    vE.inc(ii)
                block.vector(vx_)
                x8_v = vE.n

                sE_f1 = sE.n
                tE_f1 = tE.n

                def tf1_(t, x8_v=x8_v, sE_f1=sE_f1):
                    t.wait_ge(vE.sem, x8_v)
                    for nb2 in range(32):
                        if nb2 < 2:
                            t.wait_ge(sE.sem, sE_f1)
                        else:
                            t.wait_ge(sE.sem, sE_f1 + nb2 - 1)
                        for g in range(8):
                            ii = t.matmul(pp[nb2 % 2][:, 0:S2],
                                          w1_sb[:, g, nb2 * 128:(nb2 + 1) * 128],
                                          xh8[:, g, :], start=(g == 0), stop=(g == 7))
                        tE.inc(ii)
                block.tensor(tf1_)
                pv["f1mm"] = tE.n

                half_cast = {}
                sE_hf = {}
                for hh in range(2):
                    def sf1_(s, tE_f1=tE_f1, hh=hh):
                        if hh == 0:
                            if pv["vh"] is not None:
                                s.wait_ge(vE.sem, pv["vh"])
                        else:
                            s.wait_ge(vE.sem, half_cast[0])
                        for m in range(16):
                            nb2 = hh * 16 + m
                            s.wait_ge(tE.sem, tE_f1 + nb2 + 1)
                            ii = s.activation(out=hbf[:, m, :], in_=pp[nb2 % 2][:, 0:S2],
                                              func=AF.Relu, bias=b1q_sb[:, nb2:nb2 + 1],
                                              scale=1.0)
                            sE.inc(ii)
                    block.scalar(sf1_)
                    sE_hf[hh] = sE.n

                    def vh_(v, hh=hh):
                        v.wait_ge(sE.sem, sE_hf[hh])
                        if hh == 0 and pv["f2mm"] is not None:
                            v.wait_ge(tE.sem, pv["f2mm"])
                        ii = v.tensor_copy(
                            out=hf8[:, hh * 16:(hh + 1) * 16, :].rearrange("p a t -> p (a t)"),
                            in_=hbf[:].rearrange("p a t -> p (a t)"))
                        vE.inc(ii)
                    block.vector(vh_)
                    half_cast[hh] = vE.n
                pv["vh"] = vE.n
                h8_v = vE.n

                sE_f2 = sE.n
                tE_f2 = tE.n

                def tf2_(t, h8_v=h8_v, sE_f2=sE_f2):
                    t.wait_ge(vE.sem, h8_v)
                    for ob in range(8):
                        if ob < 2:
                            t.wait_ge(sE.sem, sE_f2)
                        else:
                            t.wait_ge(sE.sem, sE_f2 + ob - 1)
                        for kb in range(32):
                            ii = t.matmul(pp[2 + ob % 2][:, 0:S2],
                                          w2_sb[:, kb, ob * 128:(ob + 1) * 128],
                                          hf8[:, kb, :], start=(kb == 0), stop=(kb == 31))
                        tE.inc(ii)
                block.tensor(tf2_)
                pv["f2mm"] = tE.n

                def sf2_(s, tE_f2=tE_f2):
                    if pv["o_v"] is not None:
                        s.wait_ge(vE.sem, pv["o_v"])
                    for ob in range(8):
                        s.wait_ge(tE.sem, tE_f2 + ob + 1)
                        ii = s.activation(out=y2b[:, ob, :], in_=pp[2 + ob % 2][:, 0:S2],
                                          func=AF.Identity, bias=b2q_sb[:, ob:ob + 1],
                                          scale=1.0)
                        sE.inc(ii)
                block.scalar(sf2_)
                y2_s = sE.n

                def vd_(v, bi=bi, y2_s=y2_s):
                    v.wait_ge(sE.sem, y2_s)
                    if pv["out_d"] is not None:
                        v.wait_ge(dS.sem, pv["out_d"])
                    v.tensor_tensor(out=dr2[:].rearrange("p a t -> p (a t)"),
                                    in0=y2b[:].rearrange("p a t -> p (a t)"),
                                    in1=msk_b[bi][:].rearrange("p a t -> p (a t)"),
                                    op=ALU.mult)
                    ii = v.tensor_tensor(out=rs2[:].rearrange("p a t -> p (a t)"),
                                         in0=dr2[:].rearrange("p a t -> p (a t)"),
                                         in1=xh_b[bi][:].rearrange("p a t -> p (a t)"),
                                         op=ALU.add)
                    vE.inc(ii)
                block.vector(vd_)
                pv[("vd", gi)] = vE.n
                pv["vd"] = vE.n
                rs_v = vE.n

                sE_l2 = sE.n
                tE_l2 = tE.n

                def sc2_(s, rs_v=rs_v, tE_l2=tE_l2, gi=gi):
                    s.wait_ge(vE.sem, rs_v)
                    if pv["st2"] is not None:
                        s.wait_ge(tE.sem, pv["st2"])
                    for ob in range(8):
                        if ob >= 2:
                            s.wait_ge(tE.sem, tE_l2 + ob - 1)
                        s.copy(out=rbf2[:, ob % 2, :], in_=rs2[:, ob, :])
                        ii = s.activation(out=sbf2[:, ob % 2, :], in_=rs2[:, ob, :],
                                          func=AF.Square)
                        sE.inc(ii)
                block.scalar(sc2_)

                def ts2_(t, sE_l2=sE_l2):
                    for ob in range(8):
                        t.wait_ge(sE.sem, sE_l2 + ob + 1)
                        t.matmul(pp[4][0:1, 0:S2], ones_sb[:], rbf2[:, ob % 2, :],
                                 start=(ob == 0), stop=(ob == 7), skip_group_check=True)
                        ii = t.matmul(pp[5][0:1, 0:S2], ones_sb[:], sbf2[:, ob % 2, :],
                                      start=(ob == 0), stop=(ob == 7), skip_group_check=True)
                        tE.inc(ii)
                block.tensor(ts2_)
                pv["st2"] = tE.n
                st2 = tE.n

                def sm2_(s, st2=st2):
                    s.wait_ge(tE.sem, st2)
                    if pv["rows_d"] is not None:
                        s.wait_ge(dS.sem, pv["rows_d"])
                    s.copy(out=mrow2[:, 0:S2], in_=pp[4][0:1, 0:S2])
                    ii = s.activation(out=srow2[:], in_=pp[4][0:1, 0:S2], func=AF.Square)
                    sE.inc(ii)
                block.scalar(sm2_)
                m2_s = sE.n

                def vv2_(v, m2_s=m2_s):
                    v.wait_ge(sE.sem, m2_s)
                    ii = v.tensor_tensor(out=srow2[:], in0=pp[5][0:1, 0:S2], in1=srow2[:],
                                         op=ALU.subtract)
                    vE.inc(ii)
                block.vector(vv2_)
                var2 = vE.n

                def ss2_(s, var2=var2):
                    s.wait_ge(vE.sem, var2)
                    s.activation(out=mrow2[:, S2:2 * S2], in_=srow2[:], func=AF.Ln,
                                 bias=eps_sb[0:1, 0:1])
                    ii = s.activation(out=srow2[:], in_=mrow2[:, S2:2 * S2], func=AF.Exp,
                                      scale=negh_sb[0:1, 0:1])
                    sE.inc(ii)
                block.scalar(ss2_)
                sd2 = sE.n

                def vr2_(v, sd2=sd2):
                    v.wait_ge(sE.sem, sd2)
                    ii = v.tensor_tensor(out=mrow2[:, 0:S2], in0=mrow2[:, 0:S2],
                                         in1=srow2[:], op=ALU.mult)
                    vE.inc(ii)
                block.vector(vr2_)
                row2 = vE.n

                def yb2_(y, row2=row2):
                    y.wait_ge(vE.sem, row2)
                    y.wait_ge(dS.sem, dS.n)
                    dS.inc(y.dma_start(out=row_scr2[0:1, 0:S2], in_=mrow2[:, 0:S2]))
                    dS.inc(y.dma_start(out=row_scr2[0:1, S2:2 * S2], in_=srow2[:]))
                    y.wait_ge(dS.sem, dS.n)
                    dS.inc(y.dma_start(out=mrb2[:],
                                in_=row_scr2[0:1, 0:S2].rearrange("a t -> (a t)")
                                .partition_broadcast(P)))
                    dS.inc(y.dma_start(out=rsb2[:],
                                in_=row_scr2[0:1, S2:2 * S2].rearrange("a t -> (a t)")
                                .partition_broadcast(P)))
                block.sync(yb2_)
                pv["rows_d"] = dS.n
                bb2 = dS.n

                def vo2_(v, bb2=bb2):
                    v.wait_ge(dS.sem, bb2)
                    for ob in range(8):
                        v.tensor_tensor(out=rs2[:, ob, :], in0=rs2[:, ob, :], in1=mrb2[:],
                                        op=ALU.subtract)
                        ii = v.tensor_tensor(out=dr2[:, ob, :], in0=rs2[:, ob, :],
                                             in1=rsb2[:], op=ALU.mult)
                        vE.inc(ii)
                block.vector(vo2_)
                pv["o_v"] = vE.n
                o_v = vE.n

                def yo2_(y, b=b, tok=tok, o_v=o_v):
                    y.wait_ge(vE.sem, o_v)
                    dS.inc(y.dma_start(
                        out=out_fm[b].rearrange("(g p) t -> p g t", p=P)[:, :, tok:tok + S2],
                        in_=dr2[:]))
                block.sync(yo2_)
                pv["out_d"] = dS.n

            def yfin_(y):
                y.wait_ge(dS.sem, dS.n)
                y.wait_ge(dG.sem, dG.n)
            block.sync(yfin_)

    return nc


_NC_CACHE = {}


def kernel(L0_attn_x, L0_attn_W_in, L0_attn_conv_w, L0_attn_conv_b,
           L0_attn_W_xproj, L0_attn_log_A, L0_attn_D_param, L0_attn_W_out,
           L0_ffn_W1, L0_ffn_b1, L0_ffn_W2, L0_ffn_b2):
    x = np.asarray(L0_attn_x, np.float32)
    w_in = np.ascontiguousarray(np.asarray(L0_attn_W_in, np.float32))
    conv_w = np.asarray(L0_attn_conv_w, np.float32)
    conv_b = np.asarray(L0_attn_conv_b, np.float32)
    w_xproj = np.asarray(L0_attn_W_xproj, np.float32)
    log_a = np.asarray(L0_attn_log_A, np.float32)
    d_param = np.asarray(L0_attn_D_param, np.float32)
    w_out = np.ascontiguousarray(np.asarray(L0_attn_W_out, np.float32))
    w1 = np.asarray(L0_ffn_W1, np.float32)
    b1 = np.asarray(L0_ffn_b1, np.float32)
    w2 = np.asarray(L0_ffn_W2, np.float32)
    b2 = np.asarray(L0_ffn_b2, np.float32)

    wxp_pad = np.zeros((D, 65), np.float32)
    wxp_pad[:, 0:16] = w_xproj[:, 0:16]
    wxp_pad[:, 32:48] = w_xproj[:, 16:32]
    wxp_pad[:, 64] = w_xproj[:, 32]
    w1t = np.ascontiguousarray(w1.T)
    w2t = np.ascontiguousarray(w2.T)
    loga_rr = np.ascontiguousarray(
        log_a.reshape(8, P, NST).transpose(1, 0, 2).reshape(P, 128))
    dgconv = np.zeros((8, 4, P, P), np.float32)
    for g in range(8):
        for j in range(4):
            np.fill_diagonal(dgconv[g, j], conv_w[g * P:(g + 1) * P, 0, j])
    dgd = np.zeros((8, P, P), np.float32)
    for g in range(8):
        np.fill_diagonal(dgd[g], d_param[g * P:(g + 1) * P])
    cbcol = np.ascontiguousarray(conv_b.reshape(8, P).T)
    b1rr = np.ascontiguousarray(b1.reshape(32, P).T)
    b2rr = np.ascontiguousarray(b2.reshape(8, P).T)
    i128a = np.eye(P, dtype=ml_dtypes.bfloat16)
    onesa = np.full((P, 1), 1.0 / 1024.0, dtype=ml_dtypes.bfloat16)

    import jax
    with jax.default_device(jax.devices("cpu")[0]):
        try:
            key42 = jax.random.key(42, impl="threefry2x32")
        except TypeError:
            key42 = jax.random.key(42)
        keep = np.asarray(jax.random.bernoulli(key42, 0.8, (16, T, D)))
    maskv = np.where(keep, np.float32(1.25), np.float32(0.0)).astype(ml_dtypes.bfloat16)

    if "nc" not in _NC_CACHE:
        _NC_CACHE["nc"] = build_program()
    nc = _NC_CACHE["nc"]

    in_maps = []
    for c in range(NCORE):
        xs = np.ascontiguousarray(x[2 * c:2 * c + 2].transpose(0, 2, 1))
        ms = np.ascontiguousarray(maskv[2 * c:2 * c + 2].transpose(0, 2, 1))
        in_maps.append(dict(
            x_fm=xs, w_in=w_in, w_out=w_out, wxp=wxp_pad, w1t=w1t, w2t=w2t,
            loga_r=loga_rr, dgconv=dgconv, dgd=dgd, cbcol=cbcol, b1r=b1rr,
            b2r=b2rr, mask_fm=ms, i128=i128a, onesln=onesa,
        ))

    res = run_bass_kernel_spmd(nc, in_maps, core_ids=list(range(NCORE)))
    outs = [np.asarray(res.results[c]["out_fm"]).transpose(0, 2, 1)
            for c in range(NCORE)]
    return np.ascontiguousarray(np.concatenate(outs, axis=0))
